# version 03 (final)
# baseline (speedup 1.0000x reference)
import numpy as np
from contextlib import ExitStack

import concourse.bass as bass
import concourse.mybir as mybir
from concourse.bass_utils import run_bass_kernel_spmd

F32 = mybir.dt.float32
BF16 = mybir.dt.bfloat16
AF = mybir.ActivationFunctionType
ALU = mybir.AluOpType
AX = mybir.AxisListType

B, S, D = 2, 4096, 4096
NH, HD = 32, 128
FF = 11008
TOK = 1024
NT = TOK // 128
ND = D // 128
NR = 4
SOFT_SCALE = 1.0 / float(np.sqrt(HD))
EPS = 1e-5
NFT = FF // 128
GROUPS = [[0, 1, 2, 3], [4, 5, 6, 7]]

DC_O = 256
NDC_O = D // DC_O
DC_F = 512
NDC_F = D // DC_F
NFB = NFT // 2


class Ctr:
    def __init__(self):
        self.n = 0

    def inc(self, k=1):
        self.n += k
        return self.n


STAGES = {"p0": 0, "p1k": 1, "p1v": 2, "p1q": 3, "p2": 4, "p3a": 5, "p3b": 5,
          "p3": 5, "p4": 6, "p5": 7, "p6": 8, "full": 9}


def build_nc(stage="full", att_mode="full"):
    SN = STAGES[stage]
    CHLIM = {"p3a": 2, "p3b": 8}.get(stage, 64)
    NO_TAIL = att_mode in ("notail", "noacc")
    NO_ACC = att_mode == "noacc"
    TAIL_LVL = {"t1": 1, "t2": 2}.get(att_mode, 3)
    nc = bass.Bass("TRN2", target_bir_lowering=False, debug=False)

    X = nc.dram_tensor("X", [TOK, D], F32, kind="ExternalInput").ap()
    WQKV = nc.dram_tensor("WQKV", [96, 128, 32, 128], F32, kind="ExternalInput").ap()
    QKB = nc.dram_tensor("QKB", [128, 96], F32, kind="ExternalInput").ap()
    OBB = nc.dram_tensor("OBB", [128, D], F32, kind="ExternalInput").ap()
    WO = nc.dram_tensor("WO", [NDC_O, 128, 32, DC_O], F32, kind="ExternalInput").ap()
    WG = nc.dram_tensor("WG", [NFT, 128, 32, 128], F32, kind="ExternalInput").ap()
    WU = nc.dram_tensor("WU", [NFT, 128, 32, 128], F32, kind="ExternalInput").ap()
    WD = nc.dram_tensor("WD", [NDC_F, NFB, 128, 2, DC_F], F32, kind="ExternalInput").ap()
    Y = nc.dram_tensor("Y", [TOK, D], F32, kind="ExternalOutput").ap()

    XN_STAGE = nc.dram_tensor("XN_STAGE", [TOK, D], BF16, kind="Internal")
    KV_IN = nc.dram_tensor("KV_IN", [2 * D, TOK], BF16, kind="Internal")
    KV_OUT = nc.dram_tensor("KV_OUT", [NR * 2 * D, TOK], BF16, kind="Internal")
    XMID = nc.dram_tensor("XMID", [TOK, D], F32, kind="Internal")
    H_STAGE = nc.dram_tensor("H_STAGE", [FF, TOK], BF16, kind="Internal")

    ctx = ExitStack()
    with ctx:
        ec = ctx.enter_context
        AR_A = ec(nc.sbuf_tensor("AR_A", [128, 32, 1024], BF16))
        AR_B = ec(nc.sbuf_tensor("AR_B", [128, 32, 1024], BF16))
        WBUF = ec(nc.sbuf_tensor("WBUF", [128, 2, 32, 256], BF16))
        KVB = ec(nc.sbuf_tensor("KVB", [128, 8192], BF16))
        AR_X = ec(nc.sbuf_tensor("AR_X", [128, 4096], F32))
        AR_S = ec(nc.sbuf_tensor("AR_S", [128, 4096], BF16))
        RROW = ec(nc.sbuf_tensor("RROW", [128, 512], F32))
        ONESC = ec(nc.sbuf_tensor("ONESC", [128, 1], F32))
        ONESR = ec(nc.sbuf_tensor("ONESR", [1, 128], F32))
        STATS = ec(nc.sbuf_tensor("STATS", [128, 16], F32))
        QKB_SB = ec(nc.sbuf_tensor("QKB_SB", [128, 96], F32))

        AR_S_F = AR_S.bitcast(F32)
        KVB_F = KVB.bitcast(F32)
        WBFLAT = WBUF[:, :, :, :].rearrange("p a b c -> p (a b c)")
        PT = AR_S.ap().rearrange("p (s c) -> p s c", c=512)
        ACC = AR_S_F[:, 1024:1536]
        RBCF = AR_S_F[:, 1536:2048]
        XMR = AR_S_F[:, 1024:1536]
        YRV = KVB_F[:, 0:1024]

        PS = ec(nc.psum_tensor("PS", [128, 8, 512], F32))

        sem_sy = ec(nc.semaphore("sem_sy"))
        sem_gp = ec(nc.semaphore("sem_gp"))
        sem_pe = ec(nc.semaphore("sem_pe"))
        sem_ve = ec(nc.semaphore("sem_ve"))
        sem_sc = ec(nc.semaphore("sem_sc"))
        sem_cc = ec(nc.semaphore("sem_cc"))

        c_sy, c_gp = Ctr(), Ctr()
        c_pe, c_ve, c_sc = Ctr(), Ctr(), Ctr()

        plan = {"sy": [], "gp": [], "pe": [], "ve": [], "sc": []}

        def em(key):
            def deco(fn):
                plan[key].append(fn)
                return None
            return deco

        @em("ve")
        def _(v):
            v.memset(ONESC[:], 1.0)
            v.memset(ONESR[:], 1.0).then_inc(sem_ve, 1)
        v_ones = c_ve.inc()

        @em("sy")
        def _(sy):
            sy.dma_start(out=QKB_SB[:], in_=QKB).then_inc(sem_sy, 16)
        c_sy.inc(16)

        def _preload_set(s, name):
            from concourse.hw_specs import get_activation_tables
            for _i, (_nm, _fns) in enumerate(get_activation_tables(nc.m.arch).items()):
                if _nm == name:
                    s.add_instruction(mybir.InstLoadActFuncSet(
                        name=nc.get_next_instruction_name(),
                        act_func_set_id=_i, ins=[], outs=[]))
                    return
            raise ValueError(name)

        @em("sc")
        def _(s):
            _preload_set(s, "natural_log_exp_and_others")

        def emit_ln(src_rows, dstT):
            v_norm = {}
            v_store = {}
            for tt in range(NT):
                w_norm_prev = v_norm.get(tt - 1)

                @em("sy")
                def _(sy, tt=tt, w=w_norm_prev):
                    if w is not None:
                        sy.wait_ge(sem_sc, w)
                    sy.dma_start(out=AR_X[:], in_=src_rows(tt)).then_inc(sem_sy, 16)
                v_x = c_sy.inc(16)

                @em("sc")
                def _(s, v_x=v_x, w_st=v_store.get(tt - 1)):
                    s.wait_ge(sem_sy, v_x)
                    if w_st is not None:
                        s.wait_ge(sem_sy, w_st)
                    s.activation(AR_S[:], AR_X[:], AF.Square,
                                 accum_out=STATS[:, 1:2]).then_inc(sem_sc, 1)
                v_sq = c_sc.inc()

                _b = c_ve.n

                @em("ve")
                def _(v, v_x=v_x, v_sq=v_sq, b=_b):
                    v.wait_ge(sem_sy, v_x)
                    v.reduce_sum(STATS[:, 0:1], AR_X[:], axis=AX.X).then_inc(sem_ve, 1)
                    v.wait_ge(sem_ve, b + 1)
                    v.wait_ge(sem_sc, v_sq)
                    v.tensor_scalar_mul(STATS[:, 2:3], STATS[:, 0:1], 1.0 / D)
                    v.tensor_scalar_mul(STATS[:, 4:5], STATS[:, 1:2], 1.0 / D).then_inc(sem_ve, 1)
                    v.wait_ge(sem_ve, b + 2)
                    v.tensor_mul(STATS[:, 3:4], STATS[:, 2:3], STATS[:, 2:3]).then_inc(sem_ve, 1)
                    v.wait_ge(sem_ve, b + 3)
                    v.tensor_sub(STATS[:, 4:5], STATS[:, 4:5], STATS[:, 3:4]).then_inc(sem_ve, 1)
                    v.wait_ge(sem_ve, b + 4)
                    v.tensor_scalar_add(STATS[:, 4:5], STATS[:, 4:5], EPS).then_inc(sem_ve, 1)
                c_ve.inc(4)
                v_var = c_ve.inc()

                _bs = c_sc.n

                @em("sc")
                def _(s, v_var=v_var, b=_bs):
                    s.wait_ge(sem_ve, v_var)
                    s.activation(STATS[:, 5:6], STATS[:, 4:5], AF.Ln).then_inc(sem_sc, 1)
                    s.wait_ge(sem_sc, b + 1)
                    s.activation(STATS[:, 6:7], STATS[:, 5:6], AF.Exp, scale=-0.5).then_inc(sem_sc, 1)
                c_sc.inc()
                v_rstd = c_sc.inc()

                _b2 = c_ve.n

                @em("ve")
                def _(v, v_rstd=v_rstd, b=_b2):
                    v.wait_ge(sem_sc, v_rstd)
                    v.tensor_mul(STATS[:, 7:8], STATS[:, 2:3], STATS[:, 6:7]).then_inc(sem_ve, 1)
                    v.wait_ge(sem_ve, b + 1)
                    v.tensor_scalar_mul(STATS[:, 7:8], STATS[:, 7:8], -1.0).then_inc(sem_ve, 1)
                c_ve.inc()
                v_negmr = c_ve.inc()

                @em("sc")
                def _(s, v_negmr=v_negmr):
                    s.wait_ge(sem_ve, v_negmr)
                    s.activation(AR_S[:], AR_X[:], AF.Identity,
                                 scale=STATS[:, 6:7], bias=STATS[:, 7:8]).then_inc(sem_sc, 1)
                v_norm[tt] = c_sc.inc()

                @em("sy")
                def _(sy, v_n=v_norm[tt], tt=tt):
                    sy.wait_ge(sem_sc, v_n)
                    sy.dma_start(out=XN_STAGE.ap()[tt * 128:(tt + 1) * 128, :],
                                 in_=AR_S[:]).then_inc(sem_sy, 16)
                v_store[tt] = c_sy.inc(16)

            v_all = v_store[NT - 1]
            for dt in range(ND):
                @em("sy")
                def _(sy, dt=dt, w=v_all):
                    if dt == 0:
                        sy.wait_ge(sem_sy, w)
                    sy.dma_start(out=dstT[:, dt, :],
                                 in_=XN_STAGE.ap()[:, dt * 128:(dt + 1) * 128],
                                 transpose=True).then_inc(sem_sy, 16)
                c_sy.inc(16)
            return c_sy.n

        v_xn1T = emit_ln(lambda tt: X[tt * 128:(tt + 1) * 128, :], AR_A)

        KRANGE = range(32) if SN >= 1 else range(0)
        VRANGE = range(32, 64) if SN >= 2 else range(32, 32)
        QRANGE = range(64, 96) if SN >= 3 else range(64, 64)
        wq_slot = lambda i: WBFLAT[:, (i % 2) * 8192:(i % 2) * 8192 + 4096] \
            .rearrange("p (dt f) -> p dt f", f=128)
        v_wload = {}
        v_pe_ft = {}
        v_kevac = {}
        v_kstore = {}
        v_vevac = {}
        v_vstore = {}
        v_qevac = {}

        def emit_w_ft_load(idx, ft):
            v_prev = v_pe_ft.get(idx - 2)

            @em("gp")
            def _(g, idx=idx, ft=ft, v_prev=v_prev):
                if v_prev is not None:
                    g.wait_ge(sem_pe, v_prev)
                g.dma_start(out=wq_slot(idx), in_=WQKV[ft]).then_inc(sem_gp, 16)
            v_wload[idx] = c_gp.inc(16)

        kst = lambda i: KVB[:, 2048 + (i % 2) * 1024: 2048 + (i % 2) * 1024 + 1024]
        vst = lambda i: KVB[:, (i % 2) * 1024:(i % 2) * 1024 + 1024] \
            .rearrange("p (t i) -> p t i", i=128)

        for i in KRANGE:
            emit_w_ft_load(i, 32 + i)
            kb_base = 2 * (i % 2)

            @em("pe")
            def _(t, i=i, vw=v_wload[i], v_ev=v_kevac.get(i - 2), kb_base=kb_base):
                t.wait_ge(sem_gp, vw)
                if i == 0:
                    t.wait_ge(sem_sy, v_xn1T)
                if v_ev is not None:
                    t.wait_ge(sem_sc, v_ev)
                for qc in range(2):
                    for dt in range(ND):
                        mm = t.matmul(PS[:, kb_base + qc, :], wq_slot(i)[:, dt, :].opt(),
                                 AR_A[:, dt, qc * 512:(qc + 1) * 512].opt(),
                                 start=(dt == 0), stop=(dt == ND - 1))
                        if qc == 1 and dt == ND - 1:
                            mm.then_inc(sem_pe, 1)
            v_pe_ft[i] = c_pe.inc()

            @em("sc")
            def _(s, i=i, vp=v_pe_ft[i], w_st=v_kstore.get(i - 2), kb_base=kb_base):
                s.wait_ge(sem_pe, vp)
                if w_st is not None:
                    s.wait_ge(sem_sy, w_st)
                s.activation(kst(i), PS[:, kb_base:kb_base + 2, :], AF.Identity,
                             bias=QKB_SB[:, 32 + i:33 + i]).then_inc(sem_sc, 1)
            v_kevac[i] = c_sc.inc()

            @em("sy")
            def _(sy, i=i, ve=v_kevac[i]):
                sy.wait_ge(sem_sc, ve)
                sy.dma_start(out=KV_IN.ap()[i * 128:(i + 1) * 128, :],
                             in_=kst(i)).then_inc(sem_sy, 16)
            v_kstore[i] = c_sy.inc(16)

        for i in VRANGE:
            emit_w_ft_load(i, 32 + i)
            h = i - 32

            @em("pe")
            def _(t, i=i, vw=v_wload[i],
                  v_ev=(v_vevac.get(i - 1) if i > 32 else None)):
                t.wait_ge(sem_gp, vw)
                if i == 32:
                    t.wait_ge(sem_sc, v_kevac[31])
                if v_ev is not None:
                    t.wait_ge(sem_sc, v_ev)
                for dt in range(ND):
                    for tt in range(NT):
                        mm = t.matmul(PS[:, tt, 0:128],
                                 AR_A[:, dt, tt * 128:(tt + 1) * 128].opt(),
                                 wq_slot(i)[:, dt, :].opt(),
                                 start=(dt == 0), stop=(dt == ND - 1))
                        if dt == ND - 1 and tt == NT - 1:
                            mm.then_inc(sem_pe, 1)
            v_pe_ft[i] = c_pe.inc()

            @em("sc")
            def _(s, i=i, vp=v_pe_ft[i], w_st=v_vstore.get(i - 2)):
                s.wait_ge(sem_pe, vp)
                if w_st is not None:
                    s.wait_ge(sem_sy, w_st)
                for tt in range(NT):
                    ins = s.copy(vst(i)[:, tt, :].opt(), PS[:, tt, 0:128].opt())
                    if tt == NT - 1:
                        ins.then_inc(sem_sc, 1)
            v_vevac[i] = c_sc.inc()

            @em("sy")
            def _(sy, i=i, h=h, ve=v_vevac[i]):
                sy.wait_ge(sem_sc, ve)
                sy.dma_start(
                    out=KV_IN.ap()[D + h * 128:D + (h + 1) * 128, :]
                        .rearrange("p (t i) -> p t i", i=128),
                    in_=vst(i)).then_inc(sem_sy, 16)
            v_vstore[i] = c_sy.inc(16)

        v_contrib_done = c_sy.n

        if SN >= 4:
            @em("gp")
            def _(g, w=v_contrib_done):
                g.wait_ge(sem_sy, w)
                g.collective_compute(
                    "AllGather", ALU.bypass, replica_groups=GROUPS,
                    ins=[KV_IN.ap().opt()], outs=[KV_OUT.ap().opt()],
                ).then_inc(sem_cc, 1)

        for i in QRANGE:
            h = i - 64
            emit_w_ft_load(i, h)
            kb_base = 2 * (i % 2)

            @em("pe")
            def _(t, i=i, vw=v_wload[i], kb_base=kb_base,
                  v_ev=(v_qevac.get(i - 66) if i >= 66 else None),
                  v_vev=(v_vevac.get(i - 2) if i < 66 else None)):
                t.wait_ge(sem_gp, vw)
                if v_vev is not None:
                    t.wait_ge(sem_sc, v_vev)
                if v_ev is not None:
                    t.wait_ge(sem_sc, v_ev)
                for qc in range(2):
                    for dt in range(ND):
                        mm = t.matmul(PS[:, kb_base + qc, :], wq_slot(i)[:, dt, :].opt(),
                                 AR_A[:, dt, qc * 512:(qc + 1) * 512].opt(),
                                 start=(dt == 0), stop=(dt == ND - 1))
                        if qc == 1 and dt == ND - 1:
                            mm.then_inc(sem_pe, 1)
            v_pe_ft[i] = c_pe.inc()

            @em("sc")
            def _(s, i=i, h=h, vp=v_pe_ft[i], kb_base=kb_base):
                s.wait_ge(sem_pe, vp)
                s.activation(AR_B[:, h, :], PS[:, kb_base:kb_base + 2, :],
                             AF.Identity,
                             bias=QKB_SB[:, h:h + 1]).then_inc(sem_sc, 1)
            v_qevac[h] = c_sc.inc()

        v_kvload = {}
        v_exp = {}
        v_spe = {}
        v_pv = {}
        v_accv = {}
        v_sums = {}
        v_srow = {}
        v_b5cp = {}
        v_recipv = {}
        v_ctxtt = {}
        v_bcast = {}
        v_ctxev = {}
        n_chunks = NH * 2
        pe_tail = []
        sc_tail = []
        ve_tail = []

        def kv_slot(h):
            return KVB[:, 0:8192] if h % 2 == 0 else WBFLAT[:, 0:8192]

        def kv_load(h):
            @em("sy")
            def _(sy, h=h):
                if h == 0:
                    sy.wait_ge(sem_cc, 1)
                if h == 1:
                    sy.wait_ge(sem_pe, v_pe_ft[95])
                if h >= 2:
                    sy.wait_ge(sem_pe, v_pv[(h - 2) * 64 + 63])
                sl = kv_slot(h)
                for r in range(NR):
                    sy.dma_start(
                        out=sl[:, r * 1024:(r + 1) * 1024],
                        in_=KV_OUT.ap()[r * 2 * D + h * 128:
                                        r * 2 * D + (h + 1) * 128, :],
                    ).then_inc(sem_sy, 16)
                    sy.dma_start(
                        out=sl[:, 4096 + r * 1024:4096 + (r + 1) * 1024],
                        in_=KV_OUT.ap()[r * 2 * D + D + h * 128:
                                        r * 2 * D + D + (h + 1) * 128, :],
                    ).then_inc(sem_sy, 16)
            c_sy.inc(16 * 2 * NR)
            v_kvload[h] = c_sy.n

        if SN >= 5:
            kv_load(0)
            kv_load(1)

        def pop_pe_tail():
            kind, tch, fn = pe_tail.pop(0)
            plan["pe"].append(fn)
            if kind == "sums":
                v_sums[tch] = c_pe.inc()
            else:
                v_bcast[tch] = c_pe.inc()

        def pop_sc_tail():
            kind, tch, fn = sc_tail.pop(0)
            plan["sc"].append(fn)
            if kind == "srow":
                v_srow[tch] = c_sc.inc()
            else:
                v_b5cp[tch] = c_sc.inc()

        def pop_ve_tail():
            kind, tch, fn = ve_tail.pop(0)
            plan["ve"].append(fn)
            v_recipv[tch] = c_ve.inc()
            v_ctxtt[tch] = c_ve.inc()
            v_ctxev[tch] = c_ve.inc()

        for ch in (range(min(n_chunks, CHLIM)) if SN >= 5 else range(0)):
            h, qc = ch // 2, ch % 2
            ih = h % 2
            if qc == 0 and h + 2 < NH and (h + 2) * 2 < CHLIM:
                kv_load(h + 2)
            cb = 2 + ch % 2

            for kb in range(32):
                g = ch * 32 + kb
                sb = g % 2

                @em("pe")
                def _(t, g=g, h=h, qc=qc, kb=kb, sb=sb, ih=ih,
                      vkv=v_kvload[h],
                      vq=(v_qevac[31] if ch == 0 and kb == 0 else
                          (v_qevac[h] if qc == 0 and kb == 0 else None)),
                      vex=v_exp.get(g - 2)):
                    if kb == 0:
                        t.wait_ge(sem_sy, vkv)
                    if vq is not None:
                        t.wait_ge(sem_sc, vq)
                    if vex is not None:
                        t.wait_ge(sem_sc, vex)
                    t.matmul(PS[:, sb, :],
                             kv_slot(h)[:, kb * 128:(kb + 1) * 128].opt(),
                             AR_B[:, h, qc * 512:(qc + 1) * 512].opt(),
                             start=True, stop=True).then_inc(sem_pe, 1)
                v_spe[g] = c_pe.inc()

                if kb == 2 and pe_tail:
                    pop_pe_tail()
                if kb == 6 and pe_tail:
                    pop_pe_tail()

                @em("sc")
                def _(s, g=g, sb=sb, vs=v_spe[g],
                      vpv=v_pv.get(g - 4), vac=v_accv.get(g - 4)):
                    s.wait_ge(sem_pe, vs)
                    if vpv is not None:
                        s.wait_ge(sem_pe, vpv)
                        s.wait_ge(sem_ve, vac)
                    s.activation(PT[:, g % 4, :], PS[:, sb, :], AF.Exp,
                                 scale=SOFT_SCALE).then_inc(sem_sc, 1)
                v_exp[g] = c_sc.inc()

                if kb == 4 and sc_tail:
                    pop_sc_tail()
                if kb == 7 and sc_tail:
                    pop_sc_tail()

                @em("pe")
                def _(t, g=g, ch=ch, h=h, kb=kb, cb=cb, vex=v_exp[g],
                      vce=v_ctxev.get(ch - 2),
                      vqg=(v_qevac[31] if ch < 2 and kb == 0 else None)):
                    t.wait_ge(sem_sc, vex)
                    if kb == 0 and vce is not None:
                        t.wait_ge(sem_ve, vce)
                    if vqg is not None:
                        t.wait_ge(sem_sc, vqg)
                    t.matmul(PS[:, cb, :],
                             kv_slot(h)[:, 4096 + kb * 128:4096 + (kb + 1) * 128].opt(),
                             PT[:, g % 4, :].opt(),
                             start=(kb == 0), stop=(kb == 31)).then_inc(sem_pe, 1)
                v_pv[g] = c_pe.inc()

                @em("ve")
                def _(v, g=g, ch=ch, kb=kb, vex=v_exp[g],
                      vprev=v_accv.get(g - 1)):
                    v.wait_ge(sem_sc, vex)
                    if NO_ACC:
                        v.nop().then_inc(sem_ve, 1)
                        return
                    if kb == 0:
                        if ch > 0 and not NO_TAIL:
                            v.wait_ge(sem_pe, v_sums[ch - 1])
                        v.tensor_copy(ACC[:], PT[:, g % 4, :]).then_inc(sem_ve, 1)
                    else:
                        v.wait_ge(sem_ve, vprev)
                        v.tensor_add(ACC[:], ACC[:], PT[:, g % 4, :]).then_inc(sem_ve, 1)
                v_accv[g] = c_ve.inc()

                if kb == 8 and ve_tail:
                    pop_ve_tail()

            v_acc_last = v_accv[ch * 32 + 31]

            def mk_sums(ch=ch, v_acc_last=v_acc_last):
                def fn(t):
                    t.wait_ge(sem_ve, v_acc_last)
                    if ch == 0:
                        t.wait_ge(sem_ve, v_ones)
                    else:
                        t.wait_ge(sem_sc, v_srow[ch - 1])
                    t.matmul(PS[0:1, 4, :], ONESC[:].opt(), ACC[:].opt(),
                             start=True, stop=True).then_inc(sem_pe, 1)
                return fn

            def mk_srow(ch=ch):
                def fn(s):
                    s.wait_ge(sem_pe, v_sums[ch])
                    if ch > 0:
                        s.wait_ge(sem_pe, v_bcast[ch - 1])
                    s.copy(RROW[0:1, :].opt(), PS[0:1, 4, :].opt()).then_inc(sem_sc, 1)
                return fn

            def mk_bcast(ch=ch):
                def fn(t):
                    t.wait_ge(sem_sc, v_srow[ch])
                    if ch > 0:
                        t.wait_ge(sem_sc, v_b5cp[ch - 1])
                    t.matmul(PS[:, 5, :], ONESR[:].opt(), RROW[0:1, :].opt(),
                             start=True, stop=True).then_inc(sem_pe, 1)
                return fn

            def mk_b5cp(ch=ch):
                def fn(s):
                    s.wait_ge(sem_pe, v_bcast[ch])
                    s.copy(RROW[:].opt(), PS[:, 5, :].opt()).then_inc(sem_sc, 1)
                return fn

            def mk_fin(ch=ch, h=h, qc=qc, cb=cb):
                def fn(v):
                    v.wait_ge(sem_sc, v_b5cp[ch])
                    v.reciprocal(RBCF, RROW[:]).then_inc(sem_ve, 1)
                    v.wait_ge(sem_ve, v_recipv[ch])
                    v.wait_ge(sem_pe, v_pv[ch * 32 + 31])
                    v.tensor_tensor(AR_A[:, h, qc * 512:(qc + 1) * 512],
                                    PS[:, cb, :], RBCF, ALU.mult).then_inc(sem_ve, 1)
                    v.drain()
                    v.wait_ge(sem_ve, v_ctxtt[ch])
                    v.tensor_scalar_add(AR_A[:, h, qc * 512:(qc + 1) * 512],
                                        AR_A[:, h, qc * 512:(qc + 1) * 512],
                                        QKB_SB[:, 64 + h:65 + h]).then_inc(sem_ve, 1)
                return fn

            if NO_TAIL:
                def mk_plain(ch=ch, h=h, qc=qc, cb=cb):
                    def fn(v):
                        v.wait_ge(sem_pe, v_pv[ch * 32 + 31])
                        v.tensor_copy(AR_A[:, h, qc * 512:(qc + 1) * 512],
                                      PS[:, cb, :]).then_inc(sem_ve, 3)
                        v.drain()
                    return fn
                plan["ve"].append(mk_plain())
                v_recipv[ch] = c_ve.inc()
                v_ctxtt[ch] = c_ve.inc()
                v_ctxev[ch] = c_ve.inc()
                v_sums[ch] = c_pe.n
                v_srow[ch] = c_sc.n
                v_b5cp[ch] = c_sc.n
                v_bcast[ch] = c_pe.n
            elif TAIL_LVL < 3:
                plan["pe"].append(mk_sums())
                v_sums[ch] = c_pe.inc()
                plan["sc"].append(mk_srow())
                v_srow[ch] = c_sc.inc()
                if TAIL_LVL >= 2:
                    plan["pe"].append(mk_bcast())
                    v_bcast[ch] = c_pe.inc()
                    plan["sc"].append(mk_b5cp())
                    v_b5cp[ch] = c_sc.inc()
                else:
                    v_bcast[ch] = c_pe.n
                    v_b5cp[ch] = c_sc.n

                def mk_plain2(ch=ch, h=h, qc=qc, cb=cb):
                    def fn(v):
                        v.wait_ge(sem_pe, v_pv[ch * 32 + 31])
                        v.tensor_copy(AR_A[:, h, qc * 512:(qc + 1) * 512],
                                      PS[:, cb, :]).then_inc(sem_ve, 3)
                        v.drain()
                    return fn
                plan["ve"].append(mk_plain2())
                v_recipv[ch] = c_ve.inc()
                v_ctxtt[ch] = c_ve.inc()
                v_ctxev[ch] = c_ve.inc()
            elif ch == min(n_chunks, CHLIM) - 1:
                plan["pe"].append(mk_sums())
                v_sums[ch] = c_pe.inc()
                plan["sc"].append(mk_srow())
                v_srow[ch] = c_sc.inc()
                plan["pe"].append(mk_bcast())
                v_bcast[ch] = c_pe.inc()
                plan["sc"].append(mk_b5cp())
                v_b5cp[ch] = c_sc.inc()
                plan["ve"].append(mk_fin())
                v_recipv[ch] = c_ve.inc()
                v_ctxtt[ch] = c_ve.inc()
                v_ctxev[ch] = c_ve.inc()
            else:
                pe_tail.append(("sums", ch, mk_sums()))
                pe_tail.append(("bcast", ch, mk_bcast()))
                sc_tail.append(("srow", ch, mk_srow()))
                sc_tail.append(("b5cp", ch, mk_b5cp()))
                ve_tail.append(("fin", ch, mk_fin()))

        assert not pe_tail and not ve_tail and not sc_tail

        wo_slot = lambda i: WBUF[:, i % 2, :, :]
        xsl_slot = lambda i: AR_X.rearrange("p (r t c) -> p r t c", r=2, t=NT)[:, i % 2]
        xm_slot = lambda k: XMR[:, (k % 2) * 256:(k % 2) * 256 + 256]
        OBSB = KVB[:, 0:4096]

        v_wo = {}
        v_xsl = {}
        v_pe_o = {}
        v_xm = {}
        v_xmst = {}

        for dc in (range(NDC_O) if SN >= 6 else range(0)):
            @em("gp")
            def _(g, dc=dc, v_prev=v_pe_o.get((dc - 2, NT - 1))):
                if v_prev is not None:
                    g.wait_ge(sem_pe, v_prev)
                elif dc == 0:
                    g.wait_ge(sem_pe, v_pv[n_chunks * 32 - 1])
                elif dc == 1:
                    g.wait_ge(sem_pe, v_pe_ft[95])
                g.dma_start(out=wo_slot(dc), in_=WO[dc]).then_inc(sem_gp, 16)
            v_wo[dc] = c_gp.inc(16)

            if dc == 0:
                @em("gp")
                def _(g):
                    g.wait_ge(sem_ve, v_ctxev[n_chunks - 1])
                    g.dma_start(out=OBSB, in_=OBB).then_inc(sem_gp, 16)
                v_ob = c_gp.inc(16)

            @em("sy")
            def _(sy, dc=dc, v_prev=v_xm.get((dc - 2, NT - 1))):
                if v_prev is not None:
                    sy.wait_ge(sem_ve, v_prev)
                sy.dma_start(
                    out=xsl_slot(dc),
                    in_=X.rearrange("(t p) (dc c) -> p t dc c", p=128, c=DC_O)[:, :, dc, :],
                ).then_inc(sem_sy, 16)
            v_xsl[dc] = c_sy.inc(16)

            for tt in range(NT):
                k_lin = dc * NT + tt

                @em("pe")
                def _(t, dc=dc, tt=tt, v_evprev=v_xm.get((dc - 1, tt)),
                      first=(dc == 0 and tt == 0)):
                    if first:
                        t.wait_ge(sem_ve, v_ctxev[n_chunks - 1])
                    if tt == 0:
                        t.wait_ge(sem_gp, v_wo[dc])
                    if v_evprev is not None:
                        t.wait_ge(sem_ve, v_evprev)
                    for hh in range(NH):
                        mm = t.matmul(PS[:, tt, 0:DC_O],
                                 AR_A[:, hh, tt * 128:(tt + 1) * 128].opt(),
                                 wo_slot(dc)[:, hh, :].opt(),
                                 start=(hh == 0), stop=(hh == NH - 1))
                        if hh == NH - 1:
                            mm.then_inc(sem_pe, 1)
                v_pe_o[(dc, tt)] = c_pe.inc()

                _bx = c_ve.n

                @em("ve")
                def _(v, dc=dc, tt=tt, k_lin=k_lin, vp=v_pe_o[(dc, tt)],
                      w_st=v_xmst.get(dc * NT + tt - 2), vx=v_xsl[dc],
                      first=(dc == 0 and tt == 0), b=_bx):
                    v.wait_ge(sem_pe, vp)
                    v.wait_ge(sem_sy, vx)
                    if first:
                        v.wait_ge(sem_gp, v_ob)
                    if w_st is not None:
                        v.wait_ge(sem_sy, w_st)
                    v.tensor_tensor(xm_slot(k_lin), PS[:, tt, 0:DC_O],
                                    OBSB[:, dc * DC_O:(dc + 1) * DC_O], ALU.add).then_inc(sem_ve, 1)
                    v.drain()
                    v.wait_ge(sem_ve, b + 1)
                    v.tensor_tensor(xm_slot(k_lin), xm_slot(k_lin),
                                    xsl_slot(dc)[:, tt, :], ALU.add).then_inc(sem_ve, 1)
                c_ve.inc()
                v_xm[(dc, tt)] = c_ve.inc()

                @em("sy")
                def _(sy, dc=dc, tt=tt, k_lin=k_lin, vv=v_xm[(dc, tt)]):
                    sy.wait_ge(sem_ve, vv)
                    sy.dma_start(
                        out=XMID.ap()[tt * 128:(tt + 1) * 128,
                                      dc * DC_O:(dc + 1) * DC_O],
                        in_=xm_slot(k_lin)).then_inc(sem_sy, 16)
                v_xmst[k_lin] = c_sy.inc(16)

        v_xmid_done = c_sy.n

        if SN >= 7:
            @em("sy")
            def _(sy, w=v_xmid_done):
                sy.wait_ge(sem_sy, w)
            v_xn2T = emit_ln(lambda tt: XMID.ap()[tt * 128:(tt + 1) * 128, :], AR_B)

        if SN >= 8:
            @em("sc")
            def _(s):
                _preload_set(s, "silu_and_others")

        wg_slot = lambda i: WBFLAT[:, (i % 2) * 8192:(i % 2) * 8192 + 4096] \
            .rearrange("p (dt f) -> p dt f", f=128)
        wu_slot = lambda i: WBFLAT[:, (i % 2) * 8192 + 4096:(i % 2) * 8192 + 8192] \
            .rearrange("p (dt f) -> p dt f", f=128)
        GT = AR_S[:, 0:1024]
        UT = AR_S[:, 1024:2048]
        HT = AR_S[:, 2048:3072]

        v_wg = {}
        v_pe_f = {}
        v_gev = {}
        v_hev = {}
        v_hst = {}
        for ft in (range(NFT) if SN >= 8 else range(0)):
            fb_base = 4 * (ft % 2)

            @em("gp")
            def _(g, ft=ft, v_prev=v_pe_f.get(ft - 2),
                  v_first=v_pe_o[(NDC_O - 1, NT - 1)] if ft < 2 else None):
                if v_prev is not None:
                    g.wait_ge(sem_pe, v_prev)
                elif v_first is not None:
                    g.wait_ge(sem_pe, v_first)
                g.dma_start(out=wg_slot(ft), in_=WG[ft]).then_inc(sem_gp, 16)
                g.dma_start(out=wu_slot(ft), in_=WU[ft]).then_inc(sem_gp, 16)
            c_gp.inc(16)
            v_wg[ft] = c_gp.inc(16)

            @em("pe")
            def _(t, ft=ft, vw=v_wg[ft], fb_base=fb_base,
                  v_gprev=v_gev.get(ft - 2), v_hprev=v_hev.get(ft - 2)):
                t.wait_ge(sem_gp, vw)
                if ft == 0:
                    t.wait_ge(sem_sy, v_xn2T)
                if v_gprev is not None:
                    t.wait_ge(sem_sc, v_gprev)
                    t.wait_ge(sem_ve, v_hprev)
                for qc in range(2):
                    for dt in range(ND):
                        t.matmul(PS[:, fb_base + qc, :], wg_slot(ft)[:, dt, :].opt(),
                                 AR_B[:, dt, qc * 512:(qc + 1) * 512].opt(),
                                 start=(dt == 0), stop=(dt == ND - 1))
                for qc in range(2):
                    for dt in range(ND):
                        mm = t.matmul(PS[:, fb_base + 2 + qc, :], wu_slot(ft)[:, dt, :].opt(),
                                 AR_B[:, dt, qc * 512:(qc + 1) * 512].opt(),
                                 start=(dt == 0), stop=(dt == ND - 1))
                        if qc == 1 and dt == ND - 1:
                            mm.then_inc(sem_pe, 1)
            v_pe_f[ft] = c_pe.inc()

            @em("sc")
            def _(s, ft=ft, vp=v_pe_f[ft], fb_base=fb_base):
                s.wait_ge(sem_pe, vp)
                s.activation(GT, PS[:, fb_base:fb_base + 2, :], AF.Silu)
                s.copy(UT, PS[:, fb_base + 2:fb_base + 4, :]).then_inc(sem_sc, 1)
            v_gev[ft] = c_sc.inc()

            @em("ve")
            def _(v, ft=ft, vg=v_gev[ft], w_st=v_hst.get(ft - 1)):
                v.wait_ge(sem_sc, vg)
                if w_st is not None:
                    v.wait_ge(sem_sy, w_st)
                v.tensor_mul(HT, GT, UT).then_inc(sem_ve, 1)
            v_hev[ft] = c_ve.inc()

            @em("sy")
            def _(sy, ft=ft, vh=v_hev[ft]):
                sy.wait_ge(sem_ve, vh)
                sy.dma_start(out=H_STAGE.ap()[ft * 128:(ft + 1) * 128, :],
                             in_=HT).then_inc(sem_sy, 16)
            v_hst[ft] = c_sy.inc(16)

        WBUF_FLAT = WBUF[:, 0, :, :].rearrange("p a b -> p (a b)")
        wd_slot = lambda i: WBUF_FLAT[:, (i % 4) * 1024:(i % 4) * 1024 + 1024] \
            .rearrange("p (q c) -> p q c", c=DC_F)
        h2_slot = lambda i: AR_S.rearrange("p (r q c) -> p r q c", r=2, q=2)[:, i % 2]
        xmsl = AR_X.rearrange("p (t c) -> p t c", c=DC_F)

        v_wd = {}
        v_h2 = {}
        v_pe_d = {}
        v_yev = {}
        v_yst = {}
        v_xmsl = {}
        for dc in (range(NDC_F) if SN >= 9 else range(0)):
            @em("sy")
            def _(sy, dc=dc, v_prev=v_yev.get((dc - 1, NT - 1))):
                if v_prev is not None:
                    sy.wait_ge(sem_ve, v_prev)
                sy.dma_start(
                    out=xmsl,
                    in_=XMID.ap().rearrange("(t p) (dc c) -> p t dc c",
                                            p=128, c=DC_F)[:, :, dc, :],
                ).then_inc(sem_sy, 16)
            v_xmsl[dc] = c_sy.inc(16)

            for fb in range(NFB):
                i = dc * NFB + fb

                @em("gp")
                def _(g, dc=dc, fb=fb, i=i, v_prev=v_pe_d.get(i - 4),
                      v_first=v_pe_f[NFT - 1] if i < 4 else None):
                    if v_prev is not None:
                        g.wait_ge(sem_pe, v_prev)
                    elif v_first is not None:
                        g.wait_ge(sem_pe, v_first)
                    g.dma_start(out=wd_slot(i), in_=WD[dc, fb]).then_inc(sem_gp, 16)
                v_wd[i] = c_gp.inc(16)

                @em("sy")
                def _(sy, dc=dc, fb=fb, i=i, v_prev=v_pe_d.get(i - 2),
                      v_h=v_hst[min(2 * fb + 1, NFT - 1)]):
                    if v_prev is not None:
                        sy.wait_ge(sem_pe, v_prev)
                    sy.wait_ge(sem_sy, v_h)
                    sy.dma_start(
                        out=h2_slot(i),
                        in_=H_STAGE.ap()[fb * 256:(fb + 1) * 256, :]
                            .rearrange("(q p) c -> p q c", p=128),
                    ).then_inc(sem_sy, 16)
                v_h2[i] = c_sy.inc(16)

                @em("pe")
                def _(t, dc=dc, fb=fb, i=i, vw=v_wd[i], vh=v_h2[i],
                      v_evprev=(v_yev.get((dc - 1, NT - 1)) if fb == 0 and dc > 0 else None),
                      v_p6=( (v_gev[NFT - 1], v_hev[NFT - 1]) if i == 0 else None)):
                    t.wait_ge(sem_gp, vw)
                    t.wait_ge(sem_sy, vh)
                    if v_p6 is not None:
                        t.wait_ge(sem_sc, v_p6[0])
                        t.wait_ge(sem_ve, v_p6[1])
                    if v_evprev is not None:
                        t.wait_ge(sem_ve, v_evprev)
                    for q in range(2):
                        for tt in range(NT):
                            mm = t.matmul(PS[:, tt, :],
                                     h2_slot(i)[:, q, tt * 128:(tt + 1) * 128].opt(),
                                     wd_slot(i)[:, q, :].opt(),
                                     start=(fb == 0 and q == 0),
                                     stop=(fb == NFB - 1 and q == 1))
                            if q == 1 and tt == NT - 1:
                                mm.then_inc(sem_pe, 1)
                v_pe_d[i] = c_pe.inc()

            for tt in range(NT):
                lin = dc * NT + tt

                @em("ve")
                def _(v, dc=dc, tt=tt, lin=lin, vp=v_pe_d[dc * NFB + NFB - 1],
                      vx=v_xmsl[dc], w_st=v_yst.get(lin - 2)):
                    v.wait_ge(sem_pe, vp)
                    v.wait_ge(sem_sy, vx)
                    if w_st is not None:
                        v.wait_ge(sem_sy, w_st)
                    v.tensor_tensor(YRV[:, (lin % 2) * 512:(lin % 2) * 512 + 512],
                                    PS[:, tt, :],
                                    xmsl[:, tt, :], ALU.add).then_inc(sem_ve, 1)
                    v.drain()
                v_yev[(dc, tt)] = c_ve.inc()

                @em("sy")
                def _(sy, dc=dc, tt=tt, lin=lin, vv=v_yev[(dc, tt)]):
                    sy.wait_ge(sem_ve, vv)
                    sy.dma_start(
                        out=Y[tt * 128:(tt + 1) * 128, dc * DC_F:(dc + 1) * DC_F],
                        in_=YRV[:, (lin % 2) * 512:(lin % 2) * 512 + 512]).then_inc(sem_sy, 16)
                v_yst[lin] = c_sy.inc(16)

        if SN < 9:
            Yf = Y.rearrange("a b -> (a b)")

            @em("gp")
            def _(g, vals=(c_sy.n, c_ve.n, c_sc.n, c_pe.n)):
                g.wait_ge(sem_sy, vals[0])
                g.wait_ge(sem_ve, vals[1])
                g.wait_ge(sem_sc, vals[2])
                g.wait_ge(sem_pe, vals[3])
                if SN >= 4:
                    g.wait_ge(sem_cc, 1)

            if stage == "p0" or stage == "p5":
                chunks = [(XN_STAGE.ap()[t * 128:(t + 1) * 128, :], 4096) for t in range(NT)]
            elif stage == "p1k":
                chunks = [(KV_IN.ap()[i * 128:(i + 1) * 128, :], 1024) for i in range(32)]
            elif stage == "p1v":
                chunks = [(KV_IN.ap()[D + i * 128:D + (i + 1) * 128, :], 1024) for i in range(32)]
            elif stage == "p1q":
                chunks = [(AR_B[:, i, :], 1024) for i in range(32)]
            elif stage == "p2":
                chunks = [(KV_OUT.ap()[8192 + i * 128:8192 + (i + 1) * 128, :], 1024) for i in range(32)]
            elif stage == "p3":
                chunks = [(AR_A[:, i, :], 1024) for i in range(32)]
            elif stage == "p4":
                chunks = [(XMID.ap()[t * 128:(t + 1) * 128, :], 4096) for t in range(NT)]
            elif stage == "p6":
                chunks = [(H_STAGE.ap()[i * 128:(i + 1) * 128, :], 1024) for i in range(32)]
            else:
                chunks = []

            off = 0
            v_dump_st = None
            for src_ap, n in chunks:
                is_f32 = src_ap.dtype == F32
                if is_f32:
                    @em("gp")
                    def _(g, src_ap=src_ap, n=n, w=v_dump_st):
                        if w is not None:
                            g.wait_ge(sem_sy, w)
                        g.dma_start(out=AR_X[:, 0:n], in_=src_ap).then_inc(sem_gp, 16)
                    v_cv = c_gp.inc(16)
                    sem_done, val_done = sem_gp, v_cv
                else:
                    if src_ap.space.name == "SBUF":
                        @em("ve")
                        def _(v, src_ap=src_ap, n=n, w=v_dump_st):
                            if w is not None:
                                v.wait_ge(sem_sy, w)
                            v.tensor_copy(AR_X[:, 0:n], src_ap).then_inc(sem_ve, 1)
                        v_cv = c_ve.inc()
                    else:
                        @em("gp")
                        def _(g, src_ap=src_ap, n=n, w=v_dump_st):
                            if w is not None:
                                g.wait_ge(sem_sy, w)
                            g.dma_start(out=AR_S[:, 0:n], in_=src_ap).then_inc(sem_gp, 16)
                        v_g = c_gp.inc(16)

                        @em("ve")
                        def _(v, n=n, v_g=v_g):
                            v.wait_ge(sem_gp, v_g)
                            v.tensor_copy(AR_X[:, 0:n], AR_S[:, 0:n]).then_inc(sem_ve, 1)
                        v_cv = c_ve.inc()
                    sem_done, val_done = sem_ve, v_cv

                @em("sy")
                def _(sy, off=off, n=n, sd=sem_done, vd=val_done):
                    sy.wait_ge(sd, vd)
                    sy.dma_start(
                        out=Yf[off:off + 128 * n].rearrange("(p c) -> p c", p=128),
                        in_=AR_X[:, 0:n]).then_inc(sem_sy, 16)
                v_dump_st = c_sy.inc(16)
                off += 128 * n

        @em("sy")
        def _(sy, w=c_sy.n):
            sy.wait_ge(sem_sy, w)

        @em("gp")
        def _(g, w=c_gp.n):
            g.wait_ge(sem_gp, w)

        block = ec(nc.Block())

        @block.sync
        def _(e):
            for f in plan["sy"]:
                f(e)

        @block.gpsimd
        def _(e):
            for f in plan["gp"]:
                f(e)

        @block.tensor
        def _(e):
            for f in plan["pe"]:
                f(e)

        @block.vector
        def _(e):
            for f in plan["ve"]:
                f(e)

        @block.scalar
        def _(e):
            for f in plan["sc"]:
                f(e)

    return nc


_NC_CACHE = None


def _get_nc():
    global _NC_CACHE
    if _NC_CACHE is None:
        _NC_CACHE = build_nc()
    return _NC_CACHE


def kernel(L0_attn_x, L0_attn_attn_qkvw, L0_attn_attn_qkvb, L0_attn_attn_ow,
           L0_attn_attn_ob, L0_ffn_W_gate, L0_ffn_W_up, L0_ffn_W_down):
    x = np.asarray(L0_attn_x, dtype=np.float32)
    wqkv = np.asarray(L0_attn_attn_qkvw, dtype=np.float32)
    qkvb = np.asarray(L0_attn_attn_qkvb, dtype=np.float32)
    wo = np.asarray(L0_attn_attn_ow, dtype=np.float32)
    ob = np.asarray(L0_attn_attn_ob, dtype=np.float32)
    wg = np.asarray(L0_ffn_W_gate, dtype=np.float32)
    wu = np.asarray(L0_ffn_W_up, dtype=np.float32)
    wd = np.asarray(L0_ffn_W_down, dtype=np.float32)

    wqkv_r = np.ascontiguousarray(
        wqkv.reshape(ND, 128, 96, 128).transpose(2, 1, 0, 3))
    qkb = np.ascontiguousarray(
        np.concatenate([qkvb[0:D].reshape(32, 128).T,
                        qkvb[D:2 * D].reshape(32, 128).T,
                        qkvb[2 * D:3 * D].reshape(32, 128).T], axis=1))
    obb = np.ascontiguousarray(np.broadcast_to(ob, (128, D)))
    wo_r = np.ascontiguousarray(
        wo.reshape(NH, 128, NDC_O, DC_O).transpose(2, 1, 0, 3))
    wg_r = np.ascontiguousarray(
        wg.T.reshape(ND, 128, NFT, 128).transpose(2, 1, 0, 3))
    wu_r = np.ascontiguousarray(
        wu.T.reshape(ND, 128, NFT, 128).transpose(2, 1, 0, 3))
    wd_r = np.ascontiguousarray(
        wd.T.reshape(NFB, 2, 128, NDC_F, DC_F).transpose(3, 0, 2, 1, 4))

    shared = {
        "WQKV": wqkv_r, "QKB": qkb, "OBB": obb, "WO": wo_r,
        "WG": wg_r, "WU": wu_r, "WD": wd_r,
    }
    in_maps = []
    for c in range(8):
        g, j = c // 4, c % 4
        m = dict(shared)
        m["X"] = np.ascontiguousarray(x[g, j * TOK:(j + 1) * TOK, :])
        in_maps.append(m)

    nc = _get_nc()
    res = run_bass_kernel_spmd(nc, in_maps, core_ids=list(range(8)))

    out = np.empty((B, S, D), dtype=np.float32)
    for c in range(8):
        g, j = c // 4, c % 4
        out[g, j * TOK:(j + 1) * TOK, :] = res.results[c]["Y"]
    return out
